# version 08
# speedup vs baseline: 1.4497x; 1.0192x over previous
import math
from contextlib import ExitStack

import numpy as np
import concourse.bass as bass
import concourse.mybir as mybir
import concourse.bass_utils as _bass_utils
from concourse.bass_utils import run_bass_kernel_spmd

if not getattr(_bass_utils, "_ldw_opt_patched", False):
    _orig_run_command = _bass_utils.run_command

    def _patched_run_command(cmd, *a, **kw):
        if isinstance(cmd, (list, tuple)):
            cmd = ["--enable-ldw-opt=false" if False else c
                   for c in cmd]
        return _orig_run_command(cmd, *a, **kw)

    _bass_utils.run_command = _patched_run_command
    _bass_utils._ldw_opt_patched = True

F32 = mybir.dt.float32
BF16 = mybir.dt.bfloat16
ACT = mybir.ActivationFunctionType
ALU = mybir.AluOpType
AX = mybir.AxisListType

D = 2048
T = 2048
NDT = 16
NTT = 16
H = 16
SCALE = 1.0 / math.sqrt(128.0)
EPS_LN = 1e-5
EPS_RMS = 1.1920928955078125e-07
LAM0 = 0.8
MODCH = 128
NMOD = (6 * D) // MODCH

_DBG = None


class S:
    def __init__(self, stack, nc, name):
        self.h = stack.enter_context(nc.semaphore(name))
        self.n = 0

    def inc(self, instr, k=1):
        instr.then_inc(self.h, k)
        self.n += k
        return self.n


def build_program():
    nc = bass.Bass("TRN2", target_bir_lowering=False)

    d_xT = nc.dram_tensor("xT", [D, T], F32, kind="ExternalInput")
    d_wqT = nc.dram_tensor("wqT", [D, D], F32, kind="ExternalInput")
    d_wkT = nc.dram_tensor("wkT", [D, D], F32, kind="ExternalInput")
    d_wvT = nc.dram_tensor("wvT", [D, D], F32, kind="ExternalInput")
    d_woT = nc.dram_tensor("woT", [D, D], F32, kind="ExternalInput")
    d_WT = nc.dram_tensor("WT", [D, D], F32, kind="ExternalInput")
    d_wmod = nc.dram_tensor("wmod", [D, 6 * D], F32, kind="ExternalInput")
    d_Aef = nc.dram_tensor("Aef", [D, 256], F32, kind="ExternalInput")
    d_Bf = nc.dram_tensor("Bf", [256, D], F32, kind="ExternalInput")
    d_rw1T = nc.dram_tensor("rw1T", [D, 64], F32, kind="ExternalInput")
    d_rw2T = nc.dram_tensor("rw2T", [64, 32], F32, kind="ExternalInput")
    d_rb1 = nc.dram_tensor("rb1", [64, 1], F32, kind="ExternalInput")
    d_rb2 = nc.dram_tensor("rb2", [1, 32], F32, kind="ExternalInput")
    d_gates = nc.dram_tensor("gates", [1, 16], F32, kind="ExternalInput")
    d_biasr = nc.dram_tensor("biasr", [1, D], F32, kind="ExternalInput")
    d_c2 = nc.dram_tensor("c2", [128, 16], F32, kind="ExternalInput")
    d_lamq = nc.dram_tensor("lamq", [1, 256], F32, kind="ExternalInput")
    d_lamk = nc.dram_tensor("lamk", [1, 256], F32, kind="ExternalInput")
    d_eye = nc.dram_tensor("eye", [128, 128], F32, kind="ExternalInput")
    d_sel = nc.dram_tensor("sel", [16, 256], F32, kind="ExternalInput")
    d_out = nc.dram_tensor("outT", [D, T], F32, kind="ExternalOutput")

    st = ExitStack()
    with st:
        bufA = st.enter_context(nc.sbuf_tensor("bufA", [128, NDT * T], BF16))
        biasbf = st.enter_context(nc.sbuf_tensor("biasbf", [1, D], BF16))
        modT = st.enter_context(nc.sbuf_tensor("modT", [128, 96], F32))
        capA = st.enter_context(nc.sbuf_tensor("capA", [128, 16], F32))
        capB = st.enter_context(nc.sbuf_tensor("capB", [128, 16], F32))
        gacl = st.enter_context(nc.sbuf_tensor("gacl", [128, 16], F32))
        gbcl = st.enter_context(nc.sbuf_tensor("gbcl", [128, 16], F32))
        silu_c = st.enter_context(nc.sbuf_tensor("silu_c", [128, 16], F32))
        c2s = st.enter_context(nc.sbuf_tensor("c2s", [128, 16], F32))
        onescf = st.enter_context(nc.sbuf_tensor("onescf", [128, 1], F32))
        onescb = st.enter_context(nc.sbuf_tensor("onescb", [128, 1], BF16))
        onesrf = st.enter_context(nc.sbuf_tensor("onesrf", [1, 128], F32))
        onesrb = st.enter_context(nc.sbuf_tensor("onesrb", [1, 512], BF16))
        eps11 = st.enter_context(nc.sbuf_tensor("eps11", [1, 1], F32))
        lamrows = st.enter_context(nc.sbuf_tensor("lamrows", [1, 768], F32))
        lam11 = st.enter_context(nc.sbuf_tensor("lam11", [1, 1], F32))
        lamcol = st.enter_context(nc.sbuf_tensor("lamcol", [128, 1], F32))
        kx = st.enter_context(nc.sbuf_tensor("kx", [1, 2], F32))
        bhalf = st.enter_context(nc.sbuf_tensor("bhalf", [1, 2], F32))
        rw2stg = st.enter_context(nc.sbuf_tensor("rw2stg", [64, 32], F32))
        rw2b = st.enter_context(nc.sbuf_tensor("rw2b", [64, 32], BF16))
        rb1c = st.enter_context(nc.sbuf_tensor("rb1c", [64, 1], F32))
        growr = st.enter_context(nc.sbuf_tensor("growr", [1, 16], F32))
        grep = st.enter_context(nc.sbuf_tensor("grep", [128, 16], F32))
        rb2s = st.enter_context(nc.sbuf_tensor("rb2s", [1, 32], F32))
        gatess = st.enter_context(nc.sbuf_tensor("gatess", [1, 16], F32))
        eyef = st.enter_context(nc.sbuf_tensor("eyef", [128, 128], F32))
        self16 = st.enter_context(nc.sbuf_tensor("self16", [16, 256], F32))
        sc1 = st.enter_context(nc.sbuf_tensor("sc1", [128, 16], F32))
        sc2 = st.enter_context(nc.sbuf_tensor("sc2", [128, 16], F32))
        sc3 = st.enter_context(nc.sbuf_tensor("sc3", [128, 16], F32))
        sccol = st.enter_context(nc.sbuf_tensor("sccol", [128, 16], F32))
        wcols = st.enter_context(nc.sbuf_tensor("wcols", [128, 32], F32))
        wsig = st.enter_context(nc.sbuf_tensor("wsig", [128, 32], F32))
        rsargb = st.enter_context(nc.sbuf_tensor("rsargb", [128, 4], F32))
        sgapS = st.enter_context(nc.sbuf_tensor("sgapS", [128, 64], F32))
        sgapV = st.enter_context(nc.sbuf_tensor("sgapV", [128, 64], F32))
        rsb = st.enter_context(nc.sbuf_tensor("rsb", [128, 8], F32))

        if _DBG:
            dbtr = st.enter_context(nc.sbuf_tensor("dbtr", [128, 32 + 512], BF16))
        psS = st.enter_context(nc.psum_tensor("psS", [128, 2048], F32))
        psA = st.enter_context(nc.psum_tensor("psA", [128, 2048], F32))

        s_te = S(st, nc, "s_te")
        s_sc = S(st, nc, "s_sc")
        s_dv = S(st, nc, "s_dv")
        s_wd = S(st, nc, "s_wd")
        s_gd = S(st, nc, "s_gd")
        s_od = S(st, nc, "s_od")

        def dump(name, *tensors):
            if not _DBG or name not in _DBG:
                return
            with nc.Block() as blk:
                @blk.gpsimd
                def _(g):
                    for i, tns in enumerate(tensors):
                        dt_ = nc.dram_tensor(f"dbg_{name}_{i}", list(tns.shape),
                                             tns.dtype, kind="ExternalOutput")
                        s_gd.inc(g.dma_start(dt_.ap(), tns[:, :]), 16)
                    g.wait_ge(s_gd.h, s_gd.n)

        def sgap(s):
            s.copy(sgapS[:, 0:64], modT[:, 0:64])

        def vgap(v):
            v.tensor_copy(sgapV[:, 0:64], modT[:, 0:64])

        def dinc(eng, sem):
            eng.drain()
            eng.sem_inc(sem.h, 1)
            sem.n += 1

        def w_slice_dma(eng, sem, dramh, c0, ncols, dst, rowlen=None, nrowtiles=NDT):
            RL = rowlen if rowlen is not None else dramh.shape[1]
            for g0 in range(0, nrowtiles, 4):
                gn = min(4, nrowtiles - g0)
                src = bass.AP(dramh, g0 * 128 * RL + c0,
                              [[RL, 128], [128 * RL, gn], [1, ncols]])
                dst_ap = bass.AP(dst, g0 * ncols,
                                 [[dst.shape[1], 128], [ncols, gn], [1, ncols]])
                sem.inc(eng.dma_start(dst_ap, src), 16)

        def tile_dma(eng, sem, dramh, rt, dst, width=T):
            src = bass.AP(dramh, rt * 128 * width, [[width, 128], [1, width]])
            sem.inc(eng.dma_start(dst[:, 0:width], src), 16)

        wd_c2 = s_wd.n + 16
        with nc.Block() as blk:
            @blk.sync
            def _(sy):
                for dst, src in ((c2s[:, :], d_c2.ap()),
                                 (lamrows[0:1, 0:256], d_lamq.ap()),
                                 (lamrows[0:1, 256:512], d_lamk.ap()),
                                 (rb1c[:, :], d_rb1.ap()),
                                 (rb2s[0:1, :], d_rb2.ap()),
                                 (gatess[0:1, :], d_gates.ap()),
                                 (eyef[:, :], d_eye.ap()),
                                 (self16[0:16, :], d_sel.ap()),
                                 (rw2stg[:, :], d_rw2T.ap())):
                    s_wd.inc(sy.dma_start(dst, src), 16)
                sy.wait_ge(s_wd.h, s_wd.n)

            @blk.vector
            def _(v):
                v.memset(onescf[:, :], 1.0)
                v.memset(onescb[:, :], 1.0)
                v.memset(onesrf[0:1, :], 1.0)
                v.memset(onesrb[0:1, :], 1.0)
                v.memset(eps11[0:1, :], EPS_LN)

            @blk.scalar
            def _(s):
                s.wait_ge(s_wd.h, wd_c2)
                s.activation(silu_c[:, :], c2s[:, :], ACT.Silu)

        with nc.sbuf_tensor("xr0", [128, T], F32) as xr0, \
             nc.sbuf_tensor("xr1", [128, T], F32) as xr1, \
             nc.sbuf_tensor("xsqr", [128, T], F32) as xsqr, \
             nc.sbuf_tensor("mrow", [1, T], F32) as mrow, \
             nc.sbuf_tensor("rstdrow", [1, T], F32) as rstdrow, \
             nc.sbuf_tensor("crow", [1, 512], F32) as crow, \
             nc.sbuf_tensor("rows33", [33, 2048], F32) as rows33, \
             nc.sbuf_tensor("xsq1", [128, T], F32) as xsq1, \
             nc.sbuf_tensor("sel33", [128, 66], F32) as sel33, \
             nc.sbuf_tensor("silu_cb", [128, 16], BF16) as silu_cb:
            xring = [xr0, xr1]

            te0, sc0, dv0, wd0, gd0 = s_te.n, s_sc.n, s_dv.n, s_wd.n, s_gd.n

            dv_dots = dv0 + 1
            dv_misc = dv0 + 2
            sq_ev = [dv0 + 3 + it for it in range(NTT)]
            var_ev = dv0 + 3 + NTT
            dv_lamc = var_ev + 1
            dv_caps = var_ev + 2
            conv_ev, ev = {}, {}
            _sc = sc0
            for ch in range(NMOD):
                _sc += 1; conv_ev[ch] = _sc
                if ch >= 1:
                    _sc += 1; ev[ch - 1] = _sc
                if ch == NMOD - 5:
                    _sc += 1; shift_ev = _sc
            _sc += 1; ev[NMOD - 1] = _sc
            modT_ev = _sc + 1
            rows_sc = _sc + 2
            lamexp_sc = _sc + 3
            lamcol_ev = _sc + 4
            R, TR, ST = {}, {}, {}
            _te = te0
            for ch in range(NMOD):
                _te += 1; R[ch] = _te
                if ch >= 1:
                    _te += 1; TR[ch - 1] = _te
                if ch % 6 == 0:
                    _te += 1; ST[ch // 6] = _te
            _te += 1; TR[NMOD - 1] = _te
            lam_mm = _te + 1
            grow_mm = _te + 2
            xsq = [xsqr, xsq1]

            def wd_ch(ch):
                return wd0 + 16 + 64 * (ch + 1)

            with nc.sbuf_tensor("wm0", [128, NDT * MODCH], F32) as wm0, \
                 nc.sbuf_tensor("wm1", [128, NDT * MODCH], F32) as wm1, \
                 nc.sbuf_tensor("wmb0", [128, NDT * MODCH], BF16) as wmb0, \
                 nc.sbuf_tensor("wmb1", [128, NDT * MODCH], BF16) as wmb1:
                wm = [wm0, wm1]
                wmb = [wmb0, wmb1]
                with nc.Block() as blk:
                    @blk.sync
                    def _(sy):
                        s_wd.inc(sy.dma_start(mrow[0:1, :], d_biasr.ap()), 16)
                        for ch in range(NMOD):
                            if ch >= 2:
                                sy.wait_ge(s_sc.h, conv_ev[ch - 2])
                            w_slice_dma(sy, s_wd, d_wmod, ch * MODCH, MODCH, wm[ch % 2])

                    @blk.gpsimd
                    def _(g):
                        for it in range(NTT):
                            if it >= 2:
                                g.wait_ge(s_te.h, ST[it - 2])
                            tile_dma(g, s_gd, d_xT, it, xring[it % 2])

                    @blk.tensor
                    def _(t):
                        for ch in range(NMOD):
                            t.wait_ge(s_sc.h, conv_ev[ch])
                            if ch == 0:
                                t.wait_ge(s_dv.h, dv_misc)
                            par = ch % 2
                            pcol = 512 if par == 0 else 1024
                            for it in range(NDT):
                                ins = t.matmul(psA[0:1, pcol:pcol + MODCH],
                                               silu_cb[:, it:it + 1],
                                               wmb[par][:, it * MODCH:(it + 1) * MODCH],
                                               start=(it == 0), stop=(it == NDT - 1))
                            s_te.inc(ins)
                            if ch >= 1:
                                cp = ch - 1
                                t.wait_ge(s_sc.h, ev[cp])
                                ins = t.matmul(psA[:, cp:cp + 1],
                                               crow[0:1, (cp % 2) * 256:(cp % 2) * 256 + 128],
                                               onesrf[0:1, 0:1], start=True, stop=True)
                                s_te.inc(ins)
                            if ch % 6 == 0:
                                it = ch // 6
                                t.wait_ge(s_dv.h, sq_ev[it])
                                for j in range(4):
                                    t.matmul(psS[0:33, j * 512:(j + 1) * 512],
                                             sel33[:, 0:33],
                                             xring[it % 2][:, j * 512:(j + 1) * 512],
                                             start=(it == 0), stop=False)
                                    ins = t.matmul(psS[0:33, j * 512:(j + 1) * 512],
                                                   sel33[:, 33:66],
                                                   xsq[it % 2][:, j * 512:(j + 1) * 512],
                                                   start=False, stop=(it == NTT - 1))
                                s_te.inc(ins)
                        cp = NMOD - 1
                        t.wait_ge(s_sc.h, ev[cp])
                        ins = t.matmul(psA[:, cp:cp + 1],
                                       crow[0:1, (cp % 2) * 256:(cp % 2) * 256 + 128],
                                       onesrf[0:1, 0:1], start=True, stop=True)
                        s_te.inc(ins)
                        t.wait_ge(s_dv.h, dv_lamc)
                        ins = t.matmul(psA[:, 512:513], onesrf[0:1, 0:128],
                                       lam11[0:1, 0:1], start=True, stop=True)
                        s_te.inc(ins)
                        ins = t.matmul(psA[:, 1024:1040], onesrf[0:1, 0:128],
                                       growr[0:1, 0:16], start=True, stop=True)
                        s_te.inc(ins)

                    @blk.scalar
                    def _(s):
                        for ch in range(NMOD):
                            s.wait_ge(s_wd.h, wd_ch(ch))
                            if ch >= 2:
                                s.wait_ge(s_te.h, R[ch - 2])
                            ins = s.copy(wmb[ch % 2][:, :], wm[ch % 2][:, :])
                            s_sc.inc(ins)
                            if ch >= 1:
                                cp = ch - 1
                                par = cp % 2
                                pcol = 512 if par == 0 else 1024
                                s.wait_ge(s_te.h, R[cp])
                                s.copy(crow[0:1, par * 256:par * 256 + 128],
                                       psA[0:1, pcol:pcol + MODCH])
                                dinc(s, s_sc)
                            if ch == NMOD - 5:
                                s.wait_ge(s_te.h, ST[NTT - 1])
                                s.copy(rows33[32:33, 0:2048], psS[32:33, 0:2048])
                                s.copy(rstdrow[0:1, :], rows33[32:33, 0:2048])
                                sgap(s)
                                dinc(s, s_sc)
                        cp = NMOD - 1
                        par = cp % 2
                        pcol = 512 if par == 0 else 1024
                        s.wait_ge(s_te.h, R[cp])
                        s.copy(crow[0:1, par * 256:par * 256 + 128],
                               psA[0:1, pcol:pcol + MODCH])
                        dinc(s, s_sc)
                        s.wait_ge(s_te.h, TR[NMOD - 1])
                        s.copy(modT[:, 0:96], psA[:, 0:96])
                        dinc(s, s_sc)
                        s.wait_ge(s_dv.h, var_ev)
                        s.activation(rstdrow[0:1, :], rows33[0:1, 0:2048], ACT.Ln,
                                     bias=eps11[0:1, 0:1])
                        sgap(s)
                        ins = s.activation(rstdrow[0:1, :], rstdrow[0:1, :], ACT.Exp,
                                           scale=-0.5)
                        s_sc.inc(ins)
                        s.wait_ge(s_dv.h, dv_dots)
                        s.activation(bhalf[0:1, 0:2], kx[0:1, 0:2], ACT.Exp)
                        sgap(s)
                        dinc(s, s_sc)
                        s.wait_ge(s_te.h, grow_mm)
                        s.copy(lamcol[:, 0:1], psA[:, 512:513])
                        ins = s.copy(grep[:, 0:16], psA[:, 1024:1040])
                        s_sc.inc(ins)

                    @blk.vector
                    def _(v):
                        v.memset(sel33[:, :], 0.0)
                        v.memset(sel33[:, 0:1], 1.0)
                        v.memset(sel33[:, 33 + 32:33 + 33], 1.0)
                        v.tensor_mul(lamrows[0:1, 512:768], lamrows[0:1, 0:256],
                                     lamrows[0:1, 256:512])
                        v.reduce_sum(kx[0:1, 0:1], lamrows[0:1, 512:640], axis=AX.X)
                        v.reduce_sum(kx[0:1, 1:2], lamrows[0:1, 640:768], axis=AX.X)
                        dinc(v, s_dv)
                        v.wait_ge(s_wd.h, wd0 + 16)
                        v.tensor_add(growr[0:1, 0:16], rb2s[0:1, 0:16], gatess[0:1, 0:16])
                        v.tensor_copy(biasbf[0:1, :], mrow[0:1, :])
                        v.tensor_copy(rw2b[:, :], rw2stg[:, :])
                        v.tensor_copy(silu_cb[:, :], silu_c[:, :])
                        dinc(v, s_dv)
                        for it in range(NTT):
                            v.wait_ge(s_gd.h, gd0 + 16 * (it + 1))
                            if it >= 2:
                                v.wait_ge(s_te.h, ST[it - 2])
                            ins = v.tensor_mul(xsq[it % 2][:, :], xring[it % 2][:, :],
                                               xring[it % 2][:, :])
                            s_dv.inc(ins)
                        v.wait_ge(s_te.h, ST[NTT - 1])
                        v.tensor_scalar(mrow[0:1, :], psS[0:1, 0:2048], 1.0 / D, None,
                                        op0=ALU.mult)
                        v.tensor_mul(xsqr[0:1, :], mrow[0:1, :], mrow[0:1, :])
                        v.wait_ge(s_sc.h, shift_ev)
                        v.scalar_tensor_tensor(rows33[0:1, 0:2048], rstdrow[0:1, :],
                                               1.0 / D, xsqr[0:1, :],
                                               op0=ALU.mult, op1=ALU.subtract)
                        vgap(v)
                        dinc(v, s_dv)
                        v.wait_ge(s_sc.h, lamexp_sc)
                        v.tensor_sub(kx[0:1, 0:1], bhalf[0:1, 0:1], bhalf[0:1, 1:2])
                        vgap(v)
                        v.tensor_scalar(lam11[0:1, 0:1], kx[0:1, 0:1], LAM0, None,
                                        op0=ALU.add)
                        dinc(v, s_dv)
                        v.wait_ge(s_sc.h, modT_ev)
                        v.tensor_scalar(capA[:, :], modT[:, 16:32], 1.0, None, op0=ALU.add)
                        v.tensor_scalar(capB[:, :], modT[:, 64:80], 1.0, None, op0=ALU.add)
                        hi = (sc1[:, :], wcols[:, 0:16])
                        lo = (sc2[:, :], wcols[:, 16:32])
                        mm_ = (sc3[:, :], wsig[:, 0:16])
                        dd = (gacl[:, :], gbcl[:, :])
                        t_ = (wsig[:, 16:32], sccol[:, 0:16])
                        for i, c0 in enumerate((32, 80)):
                            v.tensor_scalar(hi[i], modT[:, c0:c0 + 16], 0.25, None, op0=ALU.max)
                        for i, c0 in enumerate((32, 80)):
                            v.tensor_scalar(lo[i], modT[:, c0:c0 + 16], -0.25, None, op0=ALU.min)
                        for i, c0 in enumerate((32, 80)):
                            v.tensor_scalar(mm_[i], modT[:, c0:c0 + 16], 0.0, None, op0=ALU.is_ge)
                        for i in range(2):
                            v.tensor_sub(dd[i], hi[i], lo[i])
                        for i in range(2):
                            v.tensor_mul(t_[i], mm_[i], dd[i])
                        for i in range(2):
                            v.tensor_add(dd[i], t_[i], lo[i])
                        dinc(v, s_dv)

            assert s_te.n == grow_mm and s_sc.n == lamcol_ev and s_dv.n == dv_caps, \
                (s_te.n, grow_mm, s_sc.n, lamcol_ev, s_dv.n, dv_caps)
            assert s_wd.n == wd_ch(NMOD - 1) and s_gd.n == gd0 + 16 * NTT
            dump("mod", modT, capA, capB, gacl, gbcl, lamcol, grep, silu_c)

            def emit_ln_apply(blk, shcol0, cap, x_from_hbm, dst, mrep, rsrep):
                te0, sc0, dv0, gd0 = s_te.n, s_sc.n, s_dv.n, s_gd.n
                rep_mm = te0 + 1
                rep_ev = sc0 + 1
                app = [dv0 + it + 1 for it in range(NTT)]

                @blk.tensor
                def _(t):
                    for j in range(4):
                        t.matmul(psS[:, j * 512:(j + 1) * 512], onesrf[0:1, 0:128],
                                 rstdrow[0:1, j * 512:(j + 1) * 512], start=True, stop=True)
                    for j in range(4):
                        ins = t.matmul(psA[:, j * 512:(j + 1) * 512], onesrf[0:1, 0:128],
                                       mrow[0:1, j * 512:(j + 1) * 512], start=True,
                                       stop=True)
                    s_te.inc(ins)

                @blk.scalar
                def _(s):
                    s.wait_ge(s_te.h, rep_mm)
                    s.copy(rsrep[:, :], psS[:, 0:2048])
                    ins = s.copy(mrep[:, :], psA[:, 0:2048])
                    s_sc.inc(ins)

                if x_from_hbm:
                    @blk.gpsimd
                    def _(g):
                        for it in range(NTT):
                            if it >= 2:
                                g.wait_ge(s_dv.h, app[it - 2])
                            tile_dma(g, s_gd, d_xT, it, xring[it % 2])

                @blk.vector
                def _(v):
                    v.wait_ge(s_sc.h, rep_ev)
                    for it in range(NTT):
                        if x_from_hbm:
                            v.wait_ge(s_gd.h, gd0 + 16 * (it + 1))
                            src = xring[it % 2][:, :]
                        else:
                            src = bufA[:, it * T:(it + 1) * T]
                        v.tensor_sub(xsqr[:, :], src, mrep[:, :])
                        v.tensor_mul(xsqr[:, :], xsqr[:, :], rsrep[:, :])
                        ins = v.tensor_scalar(dst[:, it * T:(it + 1) * T], xsqr[:, :],
                                              cap[:, it:it + 1],
                                              modT[:, shcol0 + it:shcol0 + it + 1],
                                              op0=ALU.mult, op1=ALU.add)
                        s_dv.inc(ins)

            with nc.sbuf_tensor("m_rep1", [128, T], F32) as m_rep1, \
                 nc.sbuf_tensor("rs_rep1", [128, T], F32) as rs_rep1:
                with nc.Block() as blk:
                    emit_ln_apply(blk, 0, capA, True, bufA, m_rep1, rs_rep1)
                dump("h1", bufA)

        bufB = st.enter_context(nc.sbuf_tensor("bufB", [128, NTT * T], BF16))
        sqq = st.enter_context(nc.sbuf_tensor("sqq", [128, T], BF16))

        with nc.sbuf_tensor("wstg", [128, NDT * 128], F32) as wstg, \
             nc.sbuf_tensor("wqb0", [128, NDT * 128], BF16) as wqb0, \
             nc.sbuf_tensor("wqb1", [128, NDT * 128], BF16) as wqb1, \
             nc.sbuf_tensor("wkb0", [128, NDT * 128], BF16) as wkb0, \
             nc.sbuf_tensor("wkb1", [128, NDT * 128], BF16) as wkb1, \
             nc.sbuf_tensor("wvb0", [128, NDT * 128], BF16) as wvb0, \
             nc.sbuf_tensor("wvb1", [128, NDT * 128], BF16) as wvb1, \
             nc.sbuf_tensor("qe1", [65, T], BF16) as qe1, \
             nc.sbuf_tensor("qe2", [65, T], BF16) as qe2, \
             nc.sbuf_tensor("ke1", [65, T], BF16) as ke1, \
             nc.sbuf_tensor("ke2", [65, T], BF16) as ke2, \
             nc.sbuf_tensor("vext", [128, 16 * 132], BF16) as vext, \
             nc.sbuf_tensor("ebuf", [128, 2048], BF16) as ebuf, \
             nc.sbuf_tensor("out1sb", [128, 4 * 132], F32) as out1sb, \
             nc.sbuf_tensor("negub", [128, 4 * 128], F32) as negub, \
             nc.sbuf_tensor("wtmp", [128, 512], F32) as wtmp:
            wqb = [wqb0, wqb1]
            wkb = [wkb0, wkb1]
            wvb = [wvb0, wvb1]

            wd0, dv0 = s_wd.n, s_dv.n
            cvm = [dv0 + k + 1 for k in range(3)]
            with nc.Block() as blk:
                @blk.vector
                def _(v):
                    v.memset(ke1[64:65, :], -1.0)
                    v.memset(ke2[64:65, :], -1.0)
                    v.memset(bass.AP(vext, 128, [[16 * 132, 128], [132, 16], [1, 1]]), 1.0)
                    for k, dstb in enumerate((wqb[0], wkb[0], wvb[0])):
                        v.wait_ge(s_wd.h, wd0 + 64 * (k + 1))
                        ins = v.tensor_copy(dstb[:, :], wstg[:, :])
                        s_dv.inc(ins)

                @blk.sync
                def _(sy):
                    for k, dramh in enumerate((d_wqT, d_wkT, d_wvT)):
                        if k > 0:
                            sy.wait_ge(s_dv.h, cvm[k - 1])
                        w_slice_dma(sy, s_wd, dramh, 0, 128, wstg)

            H0 = {"te": s_te.n, "sc": s_sc.n, "dv": s_dv.n, "wd": s_wd.n}
            TE_H, SC_H, DV_H, WD_H = 156, 94, 16, 192

            def hb(h):
                ncv = 0 if h == H - 1 else 3
                te0 = H0["te"] + 156 * h
                sc0 = H0["sc"] + 94 * h
                dv0 = H0["dv"] + sum((4 + (0 if hh == H - 1 else 3) + 12)
                                     for hh in range(h))
                wd0 = H0["wd"] + 192 * h
                d = {}
                d["ncv"] = ncv
                d["q_mm"] = [te0 + j + 1 for j in range(4)]
                d["k_mm"] = [te0 + 4 + j + 1 for j in range(4)]
                d["v_mm"] = [te0 + 9 + c + (1 if c >= 4 else 0) + (1 if c >= 8 else 0)
                             + (1 if c >= 12 else 0) for c in range(16)]
                d["nk1_mm"] = te0 + 13
                d["nk2_mm"] = te0 + 18
                d["nq1_mm"] = te0 + 23
                d["nq2_mm"] = te0 + 28
                teS = te0 + 28
                d["smm"] = lambda qc, m, b, teS=teS: (
                    teS + (1 if (qc * 2 + m) * 8 + b == 0 else 2 * ((qc * 2 + m) * 8 + b)))
                d["avmm"] = lambda qc, m, teS=teS: (
                    teS + (128 if (qc * 2 + m) * 8 + 7 == 63 else
                           2 * ((qc * 2 + m) * 8 + 7) + 3))
                d["scq"] = [sc0 + j + 1 for j in range(4)]
                d["sck"] = [sc0 + 4 + j + 1 for j in range(4)]
                d["scv"] = [sc0 + 9 + c for c in range(16)]
                d["sqrt1"] = sc0 + 25
                d["sqrt2"] = sc0 + 26
                scS = sc0 + 26
                d["scexp"] = lambda qc, m, b, scS=scS: scS + 17 * qc + 8 * m + b + 1
                d["scrms"] = lambda qc, scS=scS: scS + 17 * qc + 17
                d["dv_sqq"] = dv0 + 1
                d["dv_sqk"] = dv0 + 2
                d["dv_kx1"] = dv0 + 3
                d["dv_kx2"] = dv0 + 4
                d["cv2"] = [dv0 + 4 + k + 1 for k in range(ncv)]
                base = dv0 + 4 + ncv
                d["dvo1"] = lambda qc, base=base: base + 3 * qc + 1
                d["dvuc"] = lambda qc, base=base: base + 3 * qc + 2
                d["dvnorm"] = lambda qc, base=base: base + 3 * qc + 3
                d["wdm"] = [wd0 + 64 * (k + 1) for k in range(ncv)]
                return d

            HD = [hb(h) for h in range(H)]

            with nc.Block() as blk:
                @blk.tensor
                def _(t):
                    for h in range(H):
                        par = h % 2
                        d = HD[h]
                        if h > 0:
                            t.wait_ge(s_sc.h, HD[h - 1]["scexp"](3, 1, 7))
                            t.wait_ge(s_dv.h, HD[h - 1]["dvuc"](3))
                            if HD[h - 1]["ncv"]:
                                t.wait_ge(s_dv.h, HD[h - 1]["cv2"][2])
                        for j in range(4):
                            for it in range(NDT):
                                ins = t.matmul(psS[:, j * 512:(j + 1) * 512],
                                               wqb[par][:, it * 128:(it + 1) * 128],
                                               bufA[:, it * T + j * 512: it * T + (j + 1) * 512],
                                               start=(it == 0), stop=(it == NDT - 1))
                            s_te.inc(ins)
                        for j in range(4):
                            for it in range(NDT):
                                ins = t.matmul(psA[:, j * 512:(j + 1) * 512],
                                               wkb[par][:, it * 128:(it + 1) * 128],
                                               bufA[:, it * T + j * 512: it * T + (j + 1) * 512],
                                               start=(it == 0), stop=(it == NDT - 1))
                            s_te.inc(ins)

                        def vchunk(c, d=d, t=t, par=par):
                            if c == 0:
                                t.wait_ge(s_sc.h, d["scq"][3])
                            if c >= 4:
                                t.wait_ge(s_sc.h, d["scv"][c - 4])
                            bank = c % 4
                            sub = c // 4
                            ps = psS[:, bank * 512 + sub * 128: bank * 512 + sub * 128 + 128]
                            for it in range(NDT):
                                ins = t.matmul(ps,
                                               bufA[:, it * T + c * 128: it * T + (c + 1) * 128],
                                               wvb[par][:, it * 128:(it + 1) * 128],
                                               start=(it == 0), stop=(it == NDT - 1))
                            s_te.inc(ins)

                        def normrow(srcb, lo, hi, t=t):
                            for j in range(4):
                                ins = t.matmul(psA[0:1, j * 512:(j + 1) * 512],
                                               onescb[lo:hi, 0:1],
                                               srcb[lo:hi, j * 512:(j + 1) * 512],
                                               start=True, stop=True)
                            s_te.inc(ins)

                        for c in range(4):
                            vchunk(c)
                        t.wait_ge(s_dv.h, d["dv_sqk"])
                        t.wait_ge(s_sc.h, d["sck"][3])
                        normrow(sqq, 0, 64)
                        for c in range(4, 8):
                            vchunk(c)
                        t.wait_ge(s_dv.h, d["dv_kx1"])
                        normrow(sqq, 64, 128)
                        for c in range(8, 12):
                            vchunk(c)
                        t.wait_ge(s_dv.h, d["dv_kx2"])
                        t.wait_ge(s_dv.h, d["dv_sqq"])
                        normrow(ebuf, 0, 64)
                        for c in range(12, 16):
                            vchunk(c)
                        t.wait_ge(s_sc.h, d["sqrt1"])
                        normrow(ebuf, 64, 128)
                        t.wait_ge(s_sc.h, d["sqrt2"])

                        def emit_scores(i, d=d, t=t):
                            qc, m, b = i // 16, (i // 8) % 2, i % 8
                            keX = ke1 if m == 0 else ke2
                            qeX = qe1 if m == 0 else qe2
                            p = b % 2
                            if i >= 2:
                                pq, rem = divmod(i - 2, 16)
                                pm, pb = divmod(rem, 8)
                                t.wait_ge(s_sc.h, d["scexp"](pq, pm, pb))
                            for kk in range(2):
                                kt = 2 * b + kk
                                ins = t.matmul(
                                    psS[:, p * 1024 + kk * 512: p * 1024 + (kk + 1) * 512],
                                    keX[:, kt * 128:(kt + 1) * 128],
                                    qeX[:, qc * 512:(qc + 1) * 512],
                                    start=True, stop=True)
                            s_te.inc(ins)

                        def emit_av(i, d=d, t=t):
                            qc, m, b = i // 16, (i // 8) % 2, i % 8
                            p = b % 2
                            t.wait_ge(s_sc.h, d["scexp"](qc, m, b))
                            if b == 0:
                                if m == 0 and qc > 0:
                                    t.wait_ge(s_dv.h, d["dvuc"](qc - 1))
                                elif m == 1:
                                    t.wait_ge(s_dv.h, d["dvo1"](qc))
                            for kk in range(2):
                                kt = 2 * b + kk
                                for qs in range(4):
                                    ins = t.matmul(
                                        psA[:, qs * 512: qs * 512 + 129],
                                        ebuf[:, p * 1024 + kk * 512 + qs * 128:
                                             p * 1024 + kk * 512 + qs * 128 + 128],
                                        vext[:, kt * 132: kt * 132 + 129],
                                        start=(kt == 0), stop=(kt == 15))
                            s_te.inc(ins)

                        emit_scores(0)
                        for i in range(64):
                            if i + 1 < 64:
                                emit_scores(i + 1)
                            emit_av(i)

                @blk.scalar
                def _(s):
                    for h in range(H):
                        d = HD[h]
                        for j in range(4):
                            s.wait_ge(s_te.h, d["q_mm"][j])
                            s.activation(qe1[0:64, j * 512:(j + 1) * 512],
                                         psS[0:64, j * 512:(j + 1) * 512], ACT.Copy,
                                         scale=SCALE)
                            ins = s.activation(qe2[0:64, j * 512:(j + 1) * 512],
                                               psS[64:128, j * 512:(j + 1) * 512], ACT.Copy,
                                               scale=SCALE)
                            s_sc.inc(ins)
                        for j in range(4):
                            s.wait_ge(s_te.h, d["k_mm"][j])
                            s.copy(ke1[0:64, j * 512:(j + 1) * 512],
                                   psA[0:64, j * 512:(j + 1) * 512])
                            ins = s.copy(ke2[0:64, j * 512:(j + 1) * 512],
                                         psA[64:128, j * 512:(j + 1) * 512])
                            s_sc.inc(ins)
                        for c in range(16):
                            s.wait_ge(s_te.h, d["v_mm"][c])
                            bank = c % 4
                            sub = c // 4
                            ins = s.copy(vext[:, c * 132: c * 132 + 128],
                                         psS[:, bank * 512 + sub * 128:
                                             bank * 512 + sub * 128 + 128])
                            s_sc.inc(ins)
                        s.wait_ge(s_te.h, d["nq1_mm"])
                        ins = s.activation(qe1[64:65, :], psA[0:1, 0:2048], ACT.Sqrt,
                                           scale=kx[0:1, 0:1])
                        s_sc.inc(ins)
                        s.wait_ge(s_te.h, d["nq2_mm"])
                        ins = s.activation(qe2[64:65, :], psA[0:1, 0:2048], ACT.Sqrt,
                                           scale=kx[0:1, 1:2])
                        s_sc.inc(ins)
                        for qc in range(4):
                            for m in range(2):
                                for b in range(8):
                                    p = b % 2
                                    s.wait_ge(s_te.h, d["smm"](qc, m, b))
                                    ins = s.activation(ebuf[:, p * 1024:(p + 1) * 1024],
                                                       psS[:, p * 1024:(p + 1) * 1024],
                                                       ACT.Exp)
                                    s_sc.inc(ins)
                            s.wait_ge(s_dv.h, d["dvuc"](qc))
                            s.activation(rsb[:, 4:8], rsargb[:, 0:4], ACT.Ln)
                            sgap(s)
                            s.activation(rsb[:, 0:4], rsb[:, 4:8], ACT.Exp, scale=-0.5)
                            sgap(s)
                            dinc(s, s_sc)

                @blk.vector
                def _(v):
                    for h in range(H):
                        npar = (h + 1) % 2
                        d = HD[h]
                        v.wait_ge(s_sc.h, d["scq"][3])
                        v.scalar_tensor_tensor(ebuf[0:64, :], qe1[0:64, :], 1.0,
                                               qe1[0:64, :], op0=ALU.bypass, op1=ALU.mult)
                        ins = v.scalar_tensor_tensor(ebuf[64:128, :], qe2[0:64, :], 1.0,
                                                     qe2[0:64, :], op0=ALU.bypass,
                                                     op1=ALU.mult)
                        s_dv.inc(ins)
                        v.wait_ge(s_sc.h, d["sck"][3])
                        v.scalar_tensor_tensor(sqq[0:64, :], ke1[0:64, :], 1.0,
                                               ke1[0:64, :], op0=ALU.bypass, op1=ALU.mult)
                        ins = v.scalar_tensor_tensor(sqq[64:128, :], ke2[0:64, :], 1.0,
                                                     ke2[0:64, :], op0=ALU.bypass,
                                                     op1=ALU.mult)
                        s_dv.inc(ins)
                        v.wait_ge(s_te.h, d["nk1_mm"])
                        v.reduce_max(kx[0:1, 0:1], psA[0:1, 0:2048], axis=AX.X)
                        dinc(v, s_dv)
                        v.wait_ge(s_te.h, d["nk2_mm"])
                        v.reduce_max(kx[0:1, 1:2], psA[0:1, 0:2048], axis=AX.X)
                        dinc(v, s_dv)
                        for k in range(d["ncv"]):
                            v.wait_ge(s_wd.h, d["wdm"][k])
                            dstb = (wqb, wkb, wvb)[k][npar]
                            ins = v.tensor_copy(dstb[:, :], wstg[:, :])
                            s_dv.inc(ins)
                        for qc in range(4):
                            v.wait_ge(s_te.h, d["avmm"](qc, 0))
                            ins = v.tensor_copy(
                                bass.AP(out1sb, 0, [[4 * 132, 128], [132, 4], [1, 129]]),
                                bass.AP(psA, 0, [[2048, 128], [512, 4], [1, 129]]))
                            s_dv.inc(ins)
                            v.wait_ge(s_te.h, d["avmm"](qc, 1))

                            def o2_(qs):
                                return psA[:, qs * 512: qs * 512 + 128]

                            def s2c_(qs):
                                return psA[:, qs * 512 + 128: qs * 512 + 129]

                            def s1c_(qs):
                                return out1sb[:, qs * 132 + 128: qs * 132 + 129]

                            def o1_(qs):
                                return out1sb[:, qs * 132: qs * 132 + 128]

                            for qs in range(4):
                                v.tensor_scalar(sccol[:, qs:qs + 1], s1c_(qs), 1e-30,
                                                None, op0=ALU.add)
                            for qs in range(4):
                                v.tensor_scalar(sccol[:, 8 + qs:9 + qs], s2c_(qs), 1e-30,
                                                None, op0=ALU.add)
                            for qs in range(4):
                                v.reciprocal(sccol[:, 4 + qs:5 + qs], sccol[:, qs:qs + 1])
                            for qs in range(4):
                                v.reciprocal(sccol[:, 12 + qs:13 + qs],
                                             sccol[:, 8 + qs:9 + qs])
                            for qs in range(4):
                                v.tensor_scalar(sccol[:, 8 + qs:9 + qs],
                                                sccol[:, 12 + qs:13 + qs],
                                                lamcol[:, 0:1], None, op0=ALU.mult)
                            for qs in range(4):
                                v.tensor_scalar(wtmp[:, qs * 128:(qs + 1) * 128], o1_(qs),
                                                sccol[:, 4 + qs:5 + qs], None, op0=ALU.mult)
                            for qs in range(4):
                                v.scalar_tensor_tensor(negub[:, qs * 128:(qs + 1) * 128],
                                                       o2_(qs), sccol[:, 8 + qs:9 + qs],
                                                       wtmp[:, qs * 128:(qs + 1) * 128],
                                                       op0=ALU.mult, op1=ALU.subtract)
                            for qs in range(4):
                                v.scalar_tensor_tensor(wtmp[:, qs * 128:(qs + 1) * 128],
                                                       negub[:, qs * 128:(qs + 1) * 128],
                                                       1.0,
                                                       negub[:, qs * 128:(qs + 1) * 128],
                                                       op0=ALU.bypass, op1=ALU.mult,
                                                       accum_out=sccol[:, qs:qs + 1])
                            for qs in range(4):
                                v.tensor_scalar(rsargb[:, qs:qs + 1], sccol[:, qs:qs + 1],
                                                1.0 / 128.0, EPS_RMS,
                                                op0=ALU.mult, op1=ALU.add)
                            vgap(v)
                            dinc(v, s_dv)
                            v.wait_ge(s_sc.h, d["scrms"](qc))
                            if _DBG and h == 2:
                                v.tensor_copy(dbtr[:, qc * 8:qc * 8 + 4], rsb[:, 0:4])
                                v.tensor_copy(dbtr[:, qc * 8 + 4:qc * 8 + 8],
                                              rsargb[:, 0:4])
                                if qc == 0:
                                    v.tensor_copy(dbtr[:, 32:544], negub[:, 0:512])
                            for qs in range(4):
                                tt = qc * 4 + qs
                                dst = bass.AP(bufB, tt * T + h, [[NTT * T, 128], [16, 128]])
                                ins = v.tensor_scalar(dst,
                                                      negub[:, qs * 128:(qs + 1) * 128],
                                                      rsb[:, qs:qs + 1], -(1.0 - LAM0),
                                                      op0=ALU.mult, op1=ALU.mult)
                            s_dv.inc(ins)

                @blk.sync
                def _(sy):
                    for h in range(H):
                        d = HD[h]
                        for k in range(d["ncv"]):
                            if k > 0:
                                sy.wait_ge(s_dv.h, d["cv2"][k - 1])
                            elif h > 0:
                                sy.wait_ge(s_dv.h, HD[h - 1]["cv2"][2])
                            w_slice_dma(sy, s_wd, (d_wqT, d_wkT, d_wvT)[k],
                                        (h + 1) * 128, 128, wstg)

            assert s_te.n == H0["te"] + 156 * H, s_te.n
            assert s_sc.n == H0["sc"] + 94 * H, s_sc.n
            assert s_wd.n == H0["wd"] + 192 * (H - 1), s_wd.n

        with nc.sbuf_tensor("xr0b", [128, T], F32) as xr0b, \
             nc.sbuf_tensor("xr1b", [128, T], F32) as xr1b, \
             nc.sbuf_tensor("Ab", [128, 2 * T], BF16) as Ab, \
             nc.sbuf_tensor("Bfb", [128, 2 * D], BF16) as Bfb, \
             nc.sbuf_tensor("rw1b", [128, NDT * 64], BF16) as rw1b:
            xring = [xr0b, xr1b]

            te0, sc0, dv0, wd0, gd0 = s_te.n, s_sc.n, s_dv.n, s_wd.n, s_gd.n
            wo_mm = [[te0 + 4 * m + j + 1 for j in range(4)] for m in range(NDT)]
            wo_cv = [dv0 + 5 * m + 1 for m in range(NDT)]
            wo_ev = [[dv0 + 5 * m + 1 + j + 1 for j in range(4)] for m in range(NDT)]
            wo_wd = [wd0 + 64 * (m + 1) for m in range(NDT)]
            x_gd = [gd0 + 16 * (m + 1) for m in range(NDT)]
            with nc.sbuf_tensor("wostg", [128, NDT * 128], F32) as wostg, \
                 nc.sbuf_tensor("wob0", [128, NDT * 128], BF16) as wob0, \
                 nc.sbuf_tensor("wob1", [128, NDT * 128], BF16) as wob1:
                wob = [wob0, wob1]
                with nc.Block() as blk:
                    @blk.sync
                    def _(sy):
                        for m in range(NDT):
                            if m >= 1:
                                sy.wait_ge(s_dv.h, wo_cv[m - 1])
                            w_slice_dma(sy, s_wd, d_woT, m * 128, 128, wostg)

                    @blk.gpsimd
                    def _(g):
                        for m in range(NDT):
                            if m >= 2:
                                g.wait_ge(s_dv.h, wo_ev[m - 2][3])
                            tile_dma(g, s_gd, d_xT, m, xring[m % 2])

                    @blk.vector
                    def _(v):
                        for m in range(NDT):
                            v.wait_ge(s_wd.h, wo_wd[m])
                            ins = v.tensor_copy(wob[m % 2][:, :], wostg[:, :])
                            s_dv.inc(ins)
                            ps = psS if m % 2 == 0 else psA
                            for j in range(4):
                                v.wait_ge(s_te.h, wo_mm[m][j])
                                v.wait_ge(s_gd.h, x_gd[m])
                                ins = v.scalar_tensor_tensor(
                                    bufA[:, m * T + j * 512: m * T + (j + 1) * 512],
                                    ps[:, j * 512:(j + 1) * 512], gacl[:, m:m + 1],
                                    xring[m % 2][:, j * 512:(j + 1) * 512],
                                    op0=ALU.mult, op1=ALU.add)
                                s_dv.inc(ins)

                    @blk.tensor
                    def _(t):
                        for m in range(NDT):
                            t.wait_ge(s_dv.h, wo_cv[m])
                            if m >= 2:
                                t.wait_ge(s_dv.h, wo_ev[m - 2][3])
                            ps = psS if m % 2 == 0 else psA
                            for j in range(4):
                                for u in range(NTT):
                                    ins = t.matmul(
                                        ps[:, j * 512:(j + 1) * 512],
                                        wob[m % 2][:, u * 128:(u + 1) * 128],
                                        bufB[:, u * T + j * 512: u * T + (j + 1) * 512],
                                        start=(u == 0), stop=(u == NTT - 1))
                                s_te.inc(ins)

            assert s_te.n == te0 + 64 and s_dv.n == dv0 + 80
            dump("x1", bufA)

            with nc.sbuf_tensor("mrow2", [1, T], F32) as mrow, \
                 nc.sbuf_tensor("rstdrow2", [1, T], F32) as rstdrow:

                te0, sc0, dv0, gd0 = s_te.n, s_sc.n, s_dv.n, s_gd.n
                l2_sq = [dv0 + it + 1 for it in range(NTT)]
                l2_mm = [te0 + it + 1 for it in range(NTT)]
                l2_rows = dv0 + NTT + 1
                cAb = [l2_rows + 1, l2_rows + 2]
                cBf = [l2_rows + 3, l2_rows + 4]
                crw = l2_rows + 5
                gAef = [gd0 + 64, gd0 + 128]
                gBf = [gd0 + 144, gd0 + 160]
                grw = gd0 + 224
                with nc.Block() as blk:
                    @blk.gpsimd
                    def _(g):
                        w_slice_dma(g, s_gd, d_Aef, 0, 128, xring[0], rowlen=256)
                        w_slice_dma(g, s_gd, d_Aef, 128, 128, xring[1], rowlen=256)
                        g.wait_ge(s_dv.h, cAb[0])
                        tile_dma(g, s_gd, d_Bf, 0, xring[0], width=2048)
                        g.wait_ge(s_dv.h, cAb[1])
                        tile_dma(g, s_gd, d_Bf, 1, xring[1], width=2048)
                        g.wait_ge(s_dv.h, cBf[0])
                        w_slice_dma(g, s_gd, d_rw1T, 0, 64, xring[0], rowlen=64)

                    @blk.vector
                    def _(v):
                        for it in range(NTT):
                            if it >= 1:
                                v.wait_ge(s_te.h, l2_mm[it - 1])
                            ins = v.scalar_tensor_tensor(
                                sqq[:, :], bufA[:, it * T:(it + 1) * T], 1.0,
                                bufA[:, it * T:(it + 1) * T],
                                op0=ALU.bypass, op1=ALU.mult)
                            s_dv.inc(ins)
                        v.wait_ge(s_te.h, l2_mm[NTT - 1])
                        v.tensor_scalar(mrow[0:1, :], psS[0:1, 0:2048], 1.0 / D, None,
                                        op0=ALU.mult)
                        v.tensor_mul(sqq[0:1, 0:2048], mrow[0:1, :], mrow[0:1, :])
                        ins = v.scalar_tensor_tensor(rstdrow[0:1, :], psA[0:1, 0:2048],
                                                     1.0 / D, sqq[0:1, 0:2048],
                                                     op0=ALU.mult, op1=ALU.subtract)
                        s_dv.inc(ins)
                        v.wait_ge(s_gd.h, gAef[0])
                        ins = v.tensor_copy(Ab[:, 0:2048], xring[0][:, :])
                        s_dv.inc(ins)
                        v.wait_ge(s_gd.h, gAef[1])
                        ins = v.tensor_copy(Ab[:, 2048:4096], xring[1][:, :])
                        s_dv.inc(ins)
                        v.wait_ge(s_gd.h, gBf[0])
                        ins = v.tensor_copy(Bfb[:, 0:2048], xring[0][:, :])
                        s_dv.inc(ins)
                        v.wait_ge(s_gd.h, gBf[1])
                        ins = v.tensor_copy(Bfb[:, 2048:4096], xring[1][:, :])
                        s_dv.inc(ins)
                        v.wait_ge(s_gd.h, grw)
                        ins = v.tensor_copy(rw1b[:, :], xring[0][:, 0:NDT * 64])
                        s_dv.inc(ins)

                    @blk.tensor
                    def _(t):
                        for it in range(NTT):
                            t.wait_ge(s_dv.h, l2_sq[it])
                            for j in range(4):
                                t.matmul(psS[0:1, j * 512:(j + 1) * 512], onescb[:, 0:1],
                                         bufA[:, it * T + j * 512: it * T + (j + 1) * 512],
                                         start=(it == 0), stop=(it == NTT - 1))
                            for j in range(4):
                                ins = t.matmul(psA[0:1, j * 512:(j + 1) * 512],
                                               onescb[:, 0:1],
                                               sqq[:, j * 512:(j + 1) * 512],
                                               start=(it == 0), stop=(it == NTT - 1))
                            s_te.inc(ins)

                    @blk.scalar
                    def _(s):
                        s.wait_ge(s_dv.h, l2_rows)
                        s.activation(rstdrow[0:1, :], rstdrow[0:1, :], ACT.Ln,
                                     bias=eps11[0:1, 0:1])
                        ins = s.activation(rstdrow[0:1, :], rstdrow[0:1, :], ACT.Exp,
                                           scale=-0.5)
                        s_sc.inc(ins)

                assert s_dv.n == crw and s_te.n == te0 + NTT and s_gd.n == grw

                with nc.sbuf_tensor("m_rep2", [128, T], BF16) as m_rep2, \
                     nc.sbuf_tensor("rs_rep2", [128, T], BF16) as rs_rep2:
                    def emit_ln2(blk):
                        te0b, sc0b, dv0b = s_te.n, s_sc.n, s_dv.n
                        rep_mm = te0b + 1
                        rep_ev = sc0b + 1

                        @blk.tensor
                        def _(t):
                            for j in range(4):
                                t.matmul(psS[:, j * 512:(j + 1) * 512], onesrf[0:1, 0:128],
                                         rstdrow[0:1, j * 512:(j + 1) * 512],
                                         start=True, stop=True)
                            for j in range(4):
                                ins = t.matmul(psA[:, j * 512:(j + 1) * 512],
                                               onesrf[0:1, 0:128],
                                               mrow[0:1, j * 512:(j + 1) * 512],
                                               start=True, stop=True)
                            s_te.inc(ins)

                        @blk.scalar
                        def _(s):
                            s.wait_ge(s_te.h, rep_mm)
                            s.copy(rs_rep2[:, :], psS[:, 0:2048])
                            ins = s.copy(m_rep2[:, :], psA[:, 0:2048])
                            s_sc.inc(ins)

                        @blk.vector
                        def _(v):
                            v.wait_ge(s_sc.h, rep_ev)
                            scr = xring[0]
                            for it in range(NTT):
                                src = bufA[:, it * T:(it + 1) * T]
                                v.tensor_sub(scr[:, :], src, m_rep2[:, :])
                                v.tensor_mul(scr[:, :], scr[:, :], rs_rep2[:, :])
                                ins = v.tensor_scalar(bufB[:, it * T:(it + 1) * T],
                                                      scr[:, :], capB[:, it:it + 1],
                                                      modT[:, 48 + it:48 + it + 1],
                                                      op0=ALU.mult, op1=ALU.add)
                                s_dv.inc(ins)

                    with nc.Block() as blk:
                        emit_ln2(blk)
                    dump("h2", bufB)

            with nc.sbuf_tensor("wxb0", [128, T], BF16) as wxb0, \
                 nc.sbuf_tensor("wxb1", [128, T], BF16) as wxb1:
                wxb = [wxb0, wxb1]
                repb = xring
                with nc.sbuf_tensor("coeffT", [16, T], F32) as coeffT, \
                     nc.sbuf_tensor("hhb", [64, T], BF16) as hhb, \
                     nc.sbuf_tensor("lgtb", [128, 256], F32) as lgtb, \
                     nc.sbuf_tensor("eqb1", [128, 256], F32) as eqb1, \
                     nc.sbuf_tensor("eqb2", [128, 256], F32) as eqb2, \
                     nc.sbuf_tensor("cobuf", [128, 256], F32) as cobuf:
                    te0, sc0, dv0 = s_te.n, s_sc.n, s_dv.n
                    hh_mm = te0 + 1
                    lg_mm = te0 + 2
                    co_tr = te0 + 3
                    rep_mm = [te0 + 4, te0 + 5]
                    xa_mm = [te0 + 6, te0 + 7]
                    sc_hh = sc0 + 1
                    sc_sig = sc0 + 2
                    sc_coT = sc0 + 3
                    sc_rep = [sc0 + 4, sc0 + 5]
                    dv_p1 = dv0 + 1
                    dv_p2 = dv0 + 2
                    dv_wx = [dv0 + 3, dv0 + 4]
                    with nc.Block() as blk:
                        @blk.tensor
                        def _(t):
                            for j in range(4):
                                for it in range(NDT):
                                    ins = t.matmul(
                                        psS[0:64, j * 512:(j + 1) * 512],
                                        rw1b[:, it * 64:(it + 1) * 64],
                                        bufB[:, it * T + j * 512: it * T + (j + 1) * 512],
                                        start=(it == 0), stop=(it == NDT - 1))
                            s_te.inc(ins)
                            t.wait_ge(s_sc.h, sc_hh)
                            for tt in range(NTT):
                                ins = t.matmul(psA[:, tt * 16:(tt + 1) * 16],
                                               hhb[:, tt * 128:(tt + 1) * 128],
                                               rw2b[:, 0:16], start=True, stop=True)
                            s_te.inc(ins)
                            t.wait_ge(s_dv.h, dv_p2)
                            for tt in range(NTT):
                                ins = t.transpose(psS[0:16, tt * 128:(tt + 1) * 128],
                                                  cobuf[:, tt * 16:(tt + 1) * 16],
                                                  eyef[:, :])
                            s_te.inc(ins)
                            t.wait_ge(s_sc.h, sc_coT)
                            for r in range(2):
                                ps = psA if r == 0 else psS
                                for j in range(4):
                                    ins = t.matmul(ps[:, j * 512:(j + 1) * 512],
                                                   self16[0:16, r * 128:(r + 1) * 128],
                                                   coeffT[0:16, j * 512:(j + 1) * 512],
                                                   start=True, stop=True)
                                s_te.inc(ins)
                            t.wait_ge(s_sc.h, sc_rep[1])
                            for r in range(2):
                                ps = psA if r == 0 else psS
                                for j in range(4):
                                    for it in range(NDT):
                                        ins = t.matmul(
                                            ps[:, j * 512:(j + 1) * 512],
                                            Ab[:, r * T + it * 128: r * T + (it + 1) * 128],
                                            bufB[:, it * T + j * 512:
                                                 it * T + (j + 1) * 512],
                                            start=(it == 0), stop=(it == NDT - 1))
                                s_te.inc(ins)

                        @blk.scalar
                        def _(s):
                            s.wait_ge(s_te.h, hh_mm)
                            ins = s.activation(hhb[:, :], psS[0:64, 0:2048], ACT.Silu,
                                               bias=rb1c[:, 0:1])
                            s_sc.inc(ins)
                            s.wait_ge(s_dv.h, dv_p1)
                            s.activation(wsig[:, 0:32], wcols[:, 0:32], ACT.Sigmoid)
                            sgap(s)
                            dinc(s, s_sc)
                            s.wait_ge(s_te.h, co_tr)
                            ins = s.copy(coeffT[0:16, :], psS[0:16, 0:2048])
                            s_sc.inc(ins)
                            for r in range(2):
                                s.wait_ge(s_te.h, rep_mm[r])
                                ps = psA if r == 0 else psS
                                ins = s.copy(repb[r][:, :], ps[:, 0:2048])
                                s_sc.inc(ins)

                        @blk.vector
                        def _(v):
                            v.wait_ge(s_te.h, lg_mm)
                            def L_(tt):
                                return lgtb[:, tt * 16:(tt + 1) * 16]

                            def E1_(tt):
                                return eqb1[:, tt * 16:(tt + 1) * 16]

                            def E2_(tt):
                                return eqb2[:, tt * 16:(tt + 1) * 16]

                            def CO_(tt):
                                return cobuf[:, tt * 16:(tt + 1) * 16]

                            for g0 in range(0, NTT, 4):
                                G = range(g0, g0 + 4)
                                for i, tt in enumerate(G):
                                    v.tensor_add(L_(tt), psA[:, tt * 16:(tt + 1) * 16],
                                                 grep[:, 0:16])
                                for i, tt in enumerate(G):
                                    v.reduce_max(sccol[:, i:i + 1], L_(tt), axis=AX.X)
                                for i, tt in enumerate(G):
                                    v.tensor_scalar(E1_(tt), L_(tt), sccol[:, i:i + 1],
                                                    None, op0=ALU.is_equal)
                                for i, tt in enumerate(G):
                                    v.scalar_tensor_tensor(E2_(tt), E1_(tt), -1e30, L_(tt),
                                                           op0=ALU.mult, op1=ALU.add)
                                for i, tt in enumerate(G):
                                    v.reduce_max(sccol[:, 4 + i:5 + i], E2_(tt), axis=AX.X)
                                for i, tt in enumerate(G):
                                    v.tensor_scalar(CO_(tt), E2_(tt), sccol[:, 4 + i:5 + i],
                                                    None, op0=ALU.is_equal)
                                for i, tt in enumerate(G):
                                    v.tensor_sub(wcols[:, 2 * tt + 1:2 * tt + 2],
                                                 sccol[:, 4 + i:5 + i], sccol[:, i:i + 1])
                                for i, tt in enumerate(G):
                                    v.tensor_scalar(wcols[:, 2 * tt:2 * tt + 1],
                                                    wcols[:, 2 * tt + 1:2 * tt + 2],
                                                    -1.0, None, op0=ALU.mult)
                            vgap(v)
                            dinc(v, s_dv)
                            v.wait_ge(s_sc.h, sc_sig)
                            for g0 in range(0, NTT, 4):
                                G = range(g0, g0 + 4)
                                for tt in G:
                                    v.tensor_scalar(E2_(tt), E1_(tt),
                                                    wsig[:, 2 * tt:2 * tt + 1], None,
                                                    op0=ALU.mult)
                                for tt in G:
                                    v.tensor_scalar(L_(tt), CO_(tt),
                                                    wsig[:, 2 * tt + 1:2 * tt + 2],
                                                    None, op0=ALU.mult)
                                for tt in G:
                                    v.tensor_add(CO_(tt), E2_(tt), L_(tt))
                            vgap(v)
                            dinc(v, s_dv)
                            for r in range(2):
                                v.wait_ge(s_te.h, xa_mm[r])
                                ps = psA if r == 0 else psS
                                ins = v.tensor_mul(wxb[r][:, :], ps[:, 0:2048],
                                                   repb[r][:, :])
                                s_dv.inc(ins)

                    assert s_te.n == xa_mm[1] and s_sc.n == sc_rep[1]
                    assert s_dv.n == dv_wx[1]
                    dump("ffn", cobuf, hhb, wxb0, wxb1)

                te0, sc0, dv0, wd0 = s_te.n, s_sc.n, s_dv.n, s_wd.n
                f_mm = [[te0 + 4 * o + j + 1 for j in range(4)] for o in range(NDT)]
                f_cv = [dv0 + 5 * o + 1 for o in range(NDT)]
                f_ev = [[dv0 + 5 * o + 1 + j + 1 for j in range(4)] for o in range(NDT)]
                f_wd = [wd0 + 64 * (o + 1) for o in range(NDT)]
                outb = xring
                with nc.sbuf_tensor("wtstg", [128, NDT * 128], F32) as wtstg, \
                     nc.sbuf_tensor("wtb0", [128, NDT * 128], BF16) as wtb0, \
                     nc.sbuf_tensor("wtb1", [128, NDT * 128], BF16) as wtb1:
                    wtb = [wtb0, wtb1]
                    with nc.Block() as blk:
                        @blk.sync
                        def _(sy):
                            for o in range(NDT):
                                if o >= 1:
                                    sy.wait_ge(s_dv.h, f_cv[o - 1])
                                w_slice_dma(sy, s_wd, d_WT, o * 128, 128, wtstg)
                            sy.wait_ge(s_od.h, 16 * NDT)

                        @blk.vector
                        def _(v):
                            for o in range(NDT):
                                v.wait_ge(s_wd.h, f_wd[o])
                                ins = v.tensor_copy(wtb[o % 2][:, :], wtstg[:, :])
                                s_dv.inc(ins)
                                ps = psS if o % 2 == 0 else psA
                                for j in range(4):
                                    v.wait_ge(s_te.h, f_mm[o][j])
                                    if o >= 2 and j == 0:
                                        v.wait_ge(s_od.h, 16 * (o - 1))
                                    ins = v.scalar_tensor_tensor(
                                        outb[o % 2][:, j * 512:(j + 1) * 512],
                                        ps[:, j * 512:(j + 1) * 512], gbcl[:, o:o + 1],
                                        bufA[:, o * T + j * 512: o * T + (j + 1) * 512],
                                        op0=ALU.mult, op1=ALU.add)
                                    s_dv.inc(ins)

                        @blk.tensor
                        def _(t):
                            for o in range(NDT):
                                t.wait_ge(s_dv.h, f_cv[o])
                                if o >= 2:
                                    t.wait_ge(s_dv.h, f_ev[o - 2][3])
                                ps = psS if o % 2 == 0 else psA
                                for j in range(4):
                                    for it in range(NDT):
                                        t.matmul(ps[:, j * 512:(j + 1) * 512],
                                                 wtb[o % 2][:, it * 128:(it + 1) * 128],
                                                 bufB[:, it * T + j * 512:
                                                      it * T + (j + 1) * 512],
                                                 start=(it == 0), stop=False)
                                    for r in range(2):
                                        t.matmul(ps[:, j * 512:(j + 1) * 512],
                                                 Bfb[:, r * D + o * 128:
                                                     r * D + (o + 1) * 128],
                                                 wxb[r][:, j * 512:(j + 1) * 512],
                                                 start=False, stop=False)
                                    ins = t.matmul(ps[:, j * 512:(j + 1) * 512],
                                                   biasbf[0:1, o * 128:(o + 1) * 128],
                                                   onesrb[0:1, 0:512],
                                                   start=False, stop=True)
                                    s_te.inc(ins)

                        @blk.gpsimd
                        def _(g):
                            for o in range(NDT):
                                g.wait_ge(s_dv.h, f_ev[o][3])
                                dst = bass.AP(d_out, o * 128 * T, [[T, 128], [1, T]])
                                s_od.inc(g.dma_start(dst, outb[o % 2][:, :]), 16)

                assert s_te.n == te0 + 64 and s_dv.n == dv0 + 80
                assert s_od.n == 16 * NDT

    return nc


def _make_sel():
    sel = np.zeros((16, 256), dtype=np.float32)
    sel[np.arange(256) // 16, np.arange(256)] = 1.0
    return sel


def kernel(L0_attn_x, L0_attn_wq, L0_attn_wk, L0_attn_wv, L0_attn_wo,
           L0_attn_lambda_q1, L0_attn_lambda_q2, L0_attn_lambda_k1, L0_attn_lambda_k2,
           L0_ffn_W, L0_ffn_bias, L0_ffn_rw1, L0_ffn_rb1, L0_ffn_rw2, L0_ffn_rb2,
           L0_ffn_A, L0_ffn_B, L0_ffn_gates, L0_c, L0_wmod):
    f = lambda a: np.asarray(a, dtype=np.float32)
    x = f(L0_attn_x)
    c = f(L0_c)
    A = f(L0_ffn_A)
    B = f(L0_ffn_B)
    shared = {
        "wqT": np.ascontiguousarray(f(L0_attn_wq).T),
        "wkT": np.ascontiguousarray(f(L0_attn_wk).T),
        "wvT": np.ascontiguousarray(f(L0_attn_wv).T),
        "woT": np.ascontiguousarray(f(L0_attn_wo).T),
        "WT": np.ascontiguousarray(f(L0_ffn_W).T),
        "wmod": np.ascontiguousarray(f(L0_wmod)),
        "Aef": np.ascontiguousarray(A.transpose(2, 0, 1).reshape(D, 256)),
        "Bf": np.ascontiguousarray(B.transpose(0, 2, 1).reshape(256, D)),
        "rw1T": np.ascontiguousarray(f(L0_ffn_rw1).T),
        "rw2T": np.ascontiguousarray(f(L0_ffn_rw2).T),
        "rb1": np.ascontiguousarray(f(L0_ffn_rb1).reshape(64, 1)),
        "rb2": np.ascontiguousarray(f(L0_ffn_rb2).reshape(1, 32)),
        "gates": np.ascontiguousarray(f(L0_ffn_gates).reshape(1, 16)),
        "biasr": np.ascontiguousarray(f(L0_ffn_bias).reshape(1, D)),
        "lamq": np.ascontiguousarray(
            np.concatenate([f(L0_attn_lambda_q1), f(L0_attn_lambda_q2)]).reshape(1, 256)),
        "lamk": np.ascontiguousarray(
            np.concatenate([f(L0_attn_lambda_k1), f(L0_attn_lambda_k2)]).reshape(1, 256)),
        "eye": np.eye(128, dtype=np.float32),
        "sel": _make_sel(),
    }
    in_maps = []
    for b in range(8):
        m = dict(shared)
        m["xT"] = np.ascontiguousarray(x[b].T)
        m["c2"] = np.ascontiguousarray(c[b].reshape(16, 128).T)
        in_maps.append(m)

    nc = build_program()
    res = run_bass_kernel_spmd(nc, in_maps, core_ids=list(range(8)))
    out = np.stack([np.asarray(r["outT"]).T for r in res.results], axis=0)
    return out.astype(np.float32)


if __name__ == "__main__":
    nc = build_program()
    print("build ok, instructions:", nc.n_instructions())
